# version 02 (final)
# baseline (speedup 1.0000x reference)
import numpy as np
from contextlib import ExitStack

import concourse.bass as bass
import concourse.mybir as mybir
from concourse.bass_utils import run_bass_kernel_spmd

F32 = mybir.dt.float32
BF16 = mybir.dt.bfloat16
I32 = mybir.dt.int32
AF = mybir.ActivationFunctionType
ALU = mybir.AluOpType
AX = mybir.AxisListType

ENGS = ("sync", "tensor", "vector", "scalar", "gpsimd")

B_GLOBAL = 16
S = 2048
D = 1024
CORES = 8
BPC = B_GLOBAL // CORES
T = BPC * S


class Prog:

    def __init__(self, name):
        self.nc = bass.Bass("TRN2", target_bir_lowering=False)
        self.name = name
        self.ops = {e: [] for e in ENGS}
        self.cur = 0
        self.prev = None
        self._stack = ExitStack()
        self.sem = self._stack.enter_context(self.nc.semaphore("gsem"))
        self._uid = 0

    def dram(self, name, shape, dtype, kind="Internal"):
        return self.nc.dram_tensor(name, list(shape), dtype, kind=kind).ap()

    def sbuf(self, name, shape, dtype):
        return self._stack.enter_context(
            self.nc.sbuf_tensor(name, list(shape), dtype))

    def psum(self, name, shape, dtype=F32):
        return self._stack.enter_context(
            self.nc.psum_tensor(name, list(shape), dtype))

    def emit(self, eng, fname, *args, dma=False, wait=None, **kw):
        cur_before = self.cur
        if wait is None:
            wait = not (eng == "tensor" and self.prev == "tensor")
        inc = 16 if dma else 1
        sem = self.sem

        def thunk(E, fname=fname, args=args, kw=kw, cur_before=cur_before,
                  wait=wait, inc=inc):
            if wait and cur_before > 0:
                E.wait_ge(sem, cur_before)
            ins = getattr(E, fname)(*args, **kw)
            ins.then_inc(sem, inc)

        self.ops[eng].append(thunk)
        self.cur += inc
        self.prev = eng

    def barrier(self):
        tot = self.cur
        sem = self.sem
        for e in ENGS:
            def thunk(E, tot=tot):
                E.wait_ge(sem, tot)
            self.ops[e].append(thunk)
        self.prev = None

    def finalize(self):
        self.barrier()
        with self.nc.Block() as blk:
            for e in ENGS:
                ops = self.ops[e]

                def body(E, ops=ops):
                    for f in ops:
                        f(E)
                getattr(blk, e)(body)
        self._stack.close()
        return self.nc

    def dma(self, out, in_):
        self.emit("sync", "dma_start", out, in_, dma=True)

    def uid(self):
        self._uid += 1
        return self._uid


def ceil_div(a, b):
    return (a + b - 1) // b


class Ctx:

    def __init__(self, p: Prog):
        self.p = p
        nc = p.nc
        self.Bc = p.sbuf("Bc", [128, 32768], BF16)
        self.Ar = p.sbuf("Ar", [128, 4096], BF16)
        self.wf = p.sbuf("wf", [128, 512], F32)
        self.wf2 = p.sbuf("wf2", [128, 512], F32)
        self.ev = [p.sbuf(f"ev{i}", [128, 512], F32) for i in range(3)]
        self.evb = [p.sbuf(f"evb{i}", [128, 512], BF16) for i in range(2)]
        self.bc1 = p.sbuf("bc1", [128, 4096], F32)
        self.bc2 = p.sbuf("bc2", [128, 4096], F32)
        self.rows = p.sbuf("rows", [1, 4096 * 3], F32)
        self.srow = p.sbuf("srow", [1, 1024], F32)
        self.colf = p.sbuf("colf", [128, 16], F32)
        self.onesf = p.sbuf("onesf", [128, 128], F32)
        self.onesb = p.sbuf("onesb", [128, 128], BF16)
        self.biasf = p.sbuf("biasf", [128, 64], F32)
        self.ps = [p.psum(f"ps{i}", [128, 512], F32) for i in range(4)]
        self.pr = p.psum("pr", [128, 512], F32)
        self.scr = p.sbuf("scr", [128, 2048], F32)
        self.scrb = p.sbuf("scrb", [128, 2048], BF16)
        self.brow = p.sbuf("brow", [1, 1024], BF16)
        self.wint = p.sbuf("wint", [128, 512], I32)
        self.mu8 = p.sbuf("mu8", [128, 512], mybir.dt.uint8)
        p.emit("vector", "memset", self.onesf[:, :], 1.0)
        p.emit("vector", "memset", self.onesb[:, :], 1.0)

    def row(self, i, n=4096):
        return self.rows[0:1, i * 4096:i * 4096 + n]

    def cconst(self, val, slot=15, p=128):
        col = self.colf[0:p, slot:slot + 1]
        self.p.emit("vector", "memset", col, float(val))
        return col

    def bcast_row(self, row_ap, L, dst):
        p = self.p
        for n0 in range(0, L, 512):
            nw = min(512, L - n0)
            p.emit("tensor", "matmul", self.pr[:, 0:nw], self.onesf[0:1, :],
                   row_ap[0:1, n0:n0 + nw], start=True, stop=True)
            p.emit("scalar", "copy", dst[:, n0:n0 + nw], self.pr[:, 0:nw])

    def recip_row(self, dst, src, L, signed=False, add=None):
        p = self.p
        t = self.srow[0:1, 0:L] if L <= 1024 else self.row(2, L)
        s = src
        if add is not None:
            p.emit("vector", "tensor_scalar_add", dst, src, float(add))
            s = dst
        if signed:
            sg = self.srow[0:1, 512:512 + L]
            p.emit("scalar", "sign", sg, s)
            p.emit("scalar", "activation", t, s, AF.Abs)
            p.emit("scalar", "activation", t, t, AF.Ln)
            p.emit("scalar", "activation", dst, t, AF.Exp, scale=-1.0)
            p.emit("vector", "tensor_mul", dst, dst, sg)
        else:
            p.emit("scalar", "activation", t, s, AF.Ln)
            p.emit("scalar", "activation", dst, t, AF.Exp, scale=-1.0)

    def bias_col_load(self, vec_dram, m, slot=0, row_len=None):
        col = self.colf[:, slot:slot + 1]
        self.p.dma(col, vec_dram.rearrange("(a b) -> a b", b=1)[m * 128:(m + 1) * 128, :])
        return col

    def _cast_load(self, dst, src_ap, src_dtype, scale):
        p = self.p
        if src_dtype == BF16:
            p.dma(dst, src_ap)
        elif src_dtype == F32:
            p.emit("gpsimd", "dma_start", dst, src_ap, dma=True)
        else:
            w = self.wf[:, 0:dst.shape[-1]]
            it = self.wint
            p.dma(it[:, 0:dst.shape[-1]], src_ap)
            p.emit("vector", "tensor_copy", w, it[:, 0:dst.shape[-1]])
            p.emit("vector", "tensor_scalar_mul", dst, w, float(scale))

    def load_B_chunk(self, B_dram, K, N, nc0, ncw, src_dtype=F32, scale=None,
                     b_off=(0, 0)):
        KT = K // 128
        assert KT * ncw <= 32768
        for k in range(KT):
            for n0 in range(0, ncw, 512):
                nw = min(512, ncw - n0)
                dst = self.Bc[:, k * ncw + n0: k * ncw + n0 + nw]
                src = B_dram[b_off[0] + k * 128: b_off[0] + (k + 1) * 128,
                             b_off[1] + nc0 + n0: b_off[1] + nc0 + n0 + nw]
                self._cast_load(dst, src, src_dtype, scale)

        def acc(k, n0, nw):
            return self.Bc[:, k * ncw + n0: k * ncw + n0 + nw]
        return acc

    def load_A_strip(self, A_dram, m, K, src_dtype=F32, mw=128, scale=None,
                     a_off=(0, 0)):
        KT = K // 128
        for k in range(KT):
            dst = self.Ar[:, k * 128:k * 128 + mw]
            src = A_dram[a_off[0] + k * 128: a_off[0] + (k + 1) * 128,
                         a_off[1] + m * 128: a_off[1] + m * 128 + mw]
            self._cast_load(dst, src, src_dtype, scale)

        def acc(k):
            return self.Ar[:, k * 128:k * 128 + mw]
        return acc

    def mm(self, out_dram, A_dram, B_dram, M, K, N, *, a_src=F32, b_src=F32,
           a_scale=None, b_scale=None, act=None, act_scale=1.0,
           bias_col=None, bias_row=None, out_dtype=F32, out2=None,
           custom=None, a_off=(0, 0), b_off=(0, 0), out_off=(0, 0)):
        p = self.p
        KT = K // 128
        MT = M // 128
        ncw_max = (32768 // KT) // 512 * 512
        if bias_row is not None:
            p.emit("gpsimd", "dma_start", self.brow[0:1, 0:N],
                   bias_row.rearrange("(a b) -> a b", a=1), dma=True)
        nci = 0
        for nc0 in range(0, N, ncw_max):
            ncw = min(ncw_max, N - nc0)
            accB = self.load_B_chunk(B_dram, K, N, nc0, ncw, b_src, b_scale,
                                     b_off)
            for m in range(MT):
                accA = self.load_A_strip(A_dram, m, K, a_src, 128, a_scale,
                                         a_off)
                colap = None
                if bias_col is not None:
                    colap = self.bias_col_load(bias_col, m, slot=0)
                for n0 in range(0, ncw, 512):
                    nw = min(512, ncw - n0)
                    ps = self.ps[nci % 2]
                    nci += 1
                    for k in range(KT):
                        p.emit("tensor", "matmul", ps[:, 0:nw], accA(k),
                               accB(k, n0, nw), start=(k == 0),
                               stop=(k == KT - 1 and bias_row is None))
                    if bias_row is not None:
                        p.emit("tensor", "matmul", ps[:, 0:nw],
                               self.onesb[0:1, :],
                               self.brow[0:1, nc0 + n0:nc0 + n0 + nw],
                               start=False, stop=True)
                    n_abs = nc0 + n0
                    if custom is not None:
                        custom(ps[:, 0:nw], m, n_abs, nw)
                        continue
                    ot = self.evb[0] if out_dtype == BF16 else self.ev[0]
                    kwargs = {}
                    if colap is not None:
                        kwargs["bias"] = colap
                    p.emit("scalar", "activation", ot[:, 0:nw], ps[:, 0:nw],
                           act or AF.Identity, scale=float(act_scale), **kwargs)
                    p.dma(out_dram[out_off[0] + m * 128:out_off[0] + (m + 1) * 128,
                                   out_off[1] + n_abs:out_off[1] + n_abs + nw],
                          ot[:, 0:nw])
                    if out2 is not None:
                        p.emit("vector", "tensor_copy", self.evb[1][:, 0:nw],
                               ot[:, 0:nw])
                        p.dma(out2[out_off[0] + m * 128:out_off[0] + (m + 1) * 128,
                                   out_off[1] + n_abs:out_off[1] + n_abs + nw],
                              self.evb[1][:, 0:nw])

    def ln_ff(self, x_dram, out_f, out_b, T=4096, D=1024, eps=1e-5,
              rms_mul=None, scale1=None, x_src=F32):
        p = self.p
        KT = D // 128
        for n0 in range(0, T, 512):
            if rms_mul is None:
                for k in range(KT):
                    p.dma(self.wf[:, 0:512], x_dram[k * 128:(k + 1) * 128, n0:n0 + 512])
                    p.emit("tensor", "matmul", self.pr[0:1, 0:512],
                           self.onesf[:, 0:1], self.wf[:, 0:512],
                           start=(k == 0), stop=(k == KT - 1), wait=True)
                p.emit("scalar", "activation", self.row(0)[0:1, n0:n0 + 512],
                       self.pr[0:1, 0:512], AF.Identity, scale=1.0 / D)
            for k in range(KT):
                p.dma(self.wf[:, 0:512], x_dram[k * 128:(k + 1) * 128, n0:n0 + 512])
                p.emit("scalar", "square", self.wf2[:, 0:512], self.wf[:, 0:512])
                p.emit("tensor", "matmul", self.pr[0:1, 0:512],
                       self.onesf[:, 0:1], self.wf2[:, 0:512],
                       start=(k == 0), stop=(k == KT - 1), wait=True)
            sc = 1.0 if rms_mul is not None else 1.0 / D
            p.emit("scalar", "activation", self.row(1)[0:1, n0:n0 + 512],
                   self.pr[0:1, 0:512], AF.Identity, scale=sc)
        r0, r1, r2 = self.row(0), self.row(1), self.row(2)
        if rms_mul is None:
            p.emit("scalar", "square", r2, r0)
            p.emit("vector", "tensor_sub", r1, r1, r2)
        p.emit("scalar", "activation", r1, r1, AF.Sqrt,
               bias=self.cconst(eps if rms_mul is None else 0.0, slot=15, p=1))
        p.emit("scalar", "activation", r1, r1, AF.Ln)
        p.emit("scalar", "activation", r2, r1, AF.Exp, scale=-1.0)
        if rms_mul is not None:
            p.emit("vector", "tensor_scalar_mul", r2, r2, float(rms_mul))
        else:
            p.emit("vector", "tensor_mul", r0, r0, r2)
            self.bcast_row(r0, T, self.bc2)
        self.bcast_row(r2, T, self.bc1)
        for k in range(KT):
            s1col = None
            if scale1 is not None:
                s1col = self.bias_col_load(scale1, k, slot=1)
            for n0 in range(0, T, 512):
                p.dma(self.wf[:, 0:512], x_dram[k * 128:(k + 1) * 128, n0:n0 + 512])
                p.emit("vector", "tensor_mul", self.ev[0][:, 0:512],
                       self.wf[:, 0:512], self.bc1[:, n0:n0 + 512])
                if rms_mul is None:
                    p.emit("vector", "tensor_sub", self.ev[0][:, 0:512],
                           self.ev[0][:, 0:512], self.bc2[:, n0:n0 + 512])
                elif scale1 is not None:
                    p.emit("vector", "tensor_scalar_mul", self.ev[0][:, 0:512],
                           self.ev[0][:, 0:512], s1col)
                if out_f is not None:
                    p.dma(out_f[k * 128:(k + 1) * 128, n0:n0 + 512],
                          self.ev[0][:, 0:512])
                if out_b is not None:
                    p.emit("vector", "tensor_copy", self.evb[0][:, 0:512],
                           self.ev[0][:, 0:512])
                    p.dma(out_b[k * 128:(k + 1) * 128, n0:n0 + 512],
                          self.evb[0][:, 0:512])



def attn_headloop(c, qf_d, kf_d, v_d, out_d, *, causal, evac, out_dtype,
                  recip_add, recip_signed, full_minus1=False, scale=1.0):
    p = c.p
    for b in range(BPC):
        for h in range(16):
            r0, r1 = h * 64, (h + 1) * 64
            p.dma(c.Ar[0:64, 0:2048], kf_d[r0:r1, b * 2048:(b + 1) * 2048])
            for kt in range(16):
                p.dma(c.scrb[:, kt * 64:(kt + 1) * 64],
                      v_d[b * 2048 + kt * 128: b * 2048 + (kt + 1) * 128, r0:r1])
            for j in range(4):
                p.dma(c.Ar[0:64, 2048:2560],
                      qf_d[r0:r1, b * 2048 + j * 512: b * 2048 + (j + 1) * 512])
                qa = c.Ar[0:64, 2048:2560]
                ktmax = 4 * j + 4 if causal else 16
                nkt = 16 if (causal and full_minus1) else ktmax
                for kt in range(nkt):
                    pt = c.Bc[:, 16384 + kt * 512: 16384 + (kt + 1) * 512]
                    if causal and kt >= ktmax:
                        p.emit("vector", "memset", pt, -1.0)
                        continue
                    p.emit("tensor", "matmul", c.ps[2][:, 0:512],
                           c.Ar[0:64, kt * 128:(kt + 1) * 128], qa,
                           start=True, stop=True)
                    straddle = causal and (4 * j <= kt <= 4 * j + 3)
                    if evac == "exp":
                        tgt = c.evb[0][:, 0:512] if straddle else pt
                        p.emit("scalar", "activation", tgt, c.ps[2][:, 0:512],
                               AF.Exp, scale=float(scale))
                    else:
                        tgt = c.evb[0][:, 0:512] if straddle else pt
                        p.emit("scalar", "activation", tgt, c.ps[2][:, 0:512],
                               AF.Identity, scale=float(scale))
                    if straddle:
                        fill = -1.0 if full_minus1 else 0.0
                        p.emit("gpsimd", "affine_select", pt, tgt,
                               [[1, 512]], ALU.is_ge, fill,
                               base=j * 512 - kt * 128, channel_multiplier=-1)
                for kt in range(nkt):
                    pt = c.Bc[:, 16384 + kt * 512: 16384 + (kt + 1) * 512]
                    p.emit("tensor", "matmul", c.pr[0:1, 0:512],
                           c.onesb[:, 0:1], pt, start=(kt == 0),
                           stop=(kt == nkt - 1), wait=(kt == 0))
                den = c.row(0)[0:1, 0:512]
                p.emit("scalar", "copy", den, c.pr[0:1, 0:512])
                rec = c.row(1)[0:1, 0:512]
                c.recip_row(rec, den, 512, signed=recip_signed, add=recip_add)
                c.bcast_row(rec, 512, c.bc1)
                for kt in range(nkt):
                    pt = c.Bc[:, 16384 + kt * 512: 16384 + (kt + 1) * 512]
                    p.emit("tensor", "matmul", c.ps[3][0:64, 0:512],
                           c.scrb[:, kt * 64:(kt + 1) * 64], pt,
                           start=(kt == 0), stop=(kt == nkt - 1),
                           wait=(kt == 0))
                p.emit("vector", "tensor_mul", c.ev[0][0:64, 0:512],
                       c.ps[3][0:64, 0:512], c.bc1[0:64, 0:512])
                if out_dtype == BF16:
                    p.emit("vector", "tensor_copy", c.evb[1][0:64, 0:512],
                           c.ev[0][0:64, 0:512])
                    p.dma(out_d[r0:r1, b * 2048 + j * 512: b * 2048 + (j + 1) * 512],
                          c.evb[1][0:64, 0:512])
                else:
                    p.dma(out_d[r0:r1, b * 2048 + j * 512: b * 2048 + (j + 1) * 512],
                          c.ev[0][0:64, 0:512])


def combine3(c, out_d, t1, t2, t3):
    p = c.p
    for kt in range(8):
        for n0 in range(0, 4096, 512):
            s = (slice(kt * 128, (kt + 1) * 128), slice(n0, n0 + 512))
            p.dma(c.wf[:, 0:512], t1[s[0], s[1]])
            p.dma(c.wf2[:, 0:512], t2[s[0], s[1]])
            p.dma(c.ev[1][:, 0:512], t3[s[0], s[1]])
            p.emit("vector", "tensor_add", c.ev[0][:, 0:512], c.wf[:, 0:512],
                   c.wf2[:, 0:512])
            p.emit("vector", "tensor_add", c.ev[0][:, 0:512],
                   c.ev[0][:, 0:512], c.ev[1][:, 0:512])
            p.dma(out_d[s[0], s[1]], c.ev[0][:, 0:512])


def emit_l0(p, c, d):
    x = d("x", [1024, 4096], F32, "ExternalInput")
    AqT = d("AqT", [1024, 1024], F32, "ExternalInput")
    AkT = d("AkT", [1024, 1024], F32, "ExternalInput")
    AvT = d("AvT", [1024, 1024], F32, "ExternalInput")
    AoT = d("AoT", [1024, 1024], F32, "ExternalInput")
    bq = d("bq", [1024], F32, "ExternalInput")
    bk = d("bk", [1024], F32, "ExternalInput")
    bv = d("bv", [1024], F32, "ExternalInput")
    bo = d("bo", [1024], F32, "ExternalInput")
    wr = d("wr", [1024, 1], F32, "ExternalInput")
    br = d("br", [1], F32, "ExternalInput")
    W1 = d("W1", [1024, 4096], F32, "ExternalInput")
    b1 = d("b1", [4096], F32, "ExternalInput")
    W2 = d("W2", [4096, 1024], F32, "ExternalInput")
    b2 = d("b2", [1024], F32, "ExternalInput")
    x1 = d("x1", [1024, 4096], F32, "ExternalOutput")
    xn_f = d("xn_f", [1024, 4096], F32)
    xn_b = d("xn_b", [1024, 4096], BF16)
    qf = d("qf", [1024, 4096], BF16)
    kf = d("kf", [1024, 4096], BF16)
    vt = d("vt", [4096, 1024], BF16)
    ab = d("ab", [1024, 4096], BF16)
    ao = d("ao", [1024, 4096], F32)
    h0 = d("h0", [4096, 4096], BF16)
    blk = d("blk", [1024, 4096], F32)
    c.ln_ff(x, xn_f, xn_b)
    c.mm(qf, AqT, xn_b, 1024, 1024, 4096, b_src=BF16, act=AF.Exp,
         bias_col=bq, out_dtype=BF16)
    c.mm(kf, AkT, xn_b, 1024, 1024, 4096, b_src=BF16, act=AF.Exp,
         bias_col=bk, out_dtype=BF16)
    c.mm(vt, xn_b, AvT, 4096, 1024, 1024, a_src=BF16, bias_row=bv,
         out_dtype=BF16)
    attn_headloop(c, qf, kf, vt, ab, causal=True, evac="copy",
                  out_dtype=BF16, recip_add=None, recip_signed=False)
    c.mm(ao, AoT, ab, 1024, 1024, 4096, b_src=BF16, bias_col=bo)
    accB = c.load_B_chunk(xn_b, 1024, 4096, 0, 4096, BF16)
    accA = c.load_A_strip(wr, 0, 1024, F32, mw=1)
    p.dma(c.colf[0:1, 2:3], br.rearrange("(a b) -> a b", b=1))
    rt = c.row(0)
    for n0 in range(0, 4096, 512):
        for k in range(8):
            p.emit("tensor", "matmul", c.pr[0:1, 0:512], accA(k),
                   accB(k, n0, 512), start=(k == 0), stop=(k == 7),
                   wait=(k == 0))
        p.emit("scalar", "activation", rt[0:1, n0:n0 + 512],
               c.pr[0:1, 0:512], AF.Identity, bias=c.colf[0:1, 2:3])
    mrow = c.row(1)
    lo, hi, mid, cnt, cond, ci, tmp = [c.srow[0:1, i:i + 1] for i in range(7)]
    trow = c.scr[0:1, 0:2048]
    for b in range(BPC):
        rsl = rt[0:1, b * 2048:(b + 1) * 2048]
        p.emit("vector", "reduce_max", hi, rsl, AX.X)
        p.emit("vector", "tensor_scalar_mul", trow, rsl, -1.0)
        p.emit("vector", "reduce_max", lo, trow, AX.X)
        p.emit("vector", "tensor_scalar", lo, lo, -1.0, -1.0, ALU.mult, ALU.add)
        for _ in range(46):
            p.emit("vector", "tensor_add", mid, lo, hi)
            p.emit("vector", "tensor_scalar_mul", mid, mid, 0.5)
            p.emit("vector", "tensor_single_scalar", trow, rsl, mid, ALU.is_gt)
            p.emit("vector", "reduce_sum", cnt, trow, AX.X)
            p.emit("vector", "tensor_single_scalar", cond, cnt, 1024.0, ALU.is_ge)
            p.emit("vector", "tensor_scalar", ci, cond, -1.0, 1.0, ALU.mult, ALU.add)
            p.emit("vector", "tensor_sub", tmp, mid, lo)
            p.emit("vector", "tensor_mul", tmp, tmp, cond)
            p.emit("vector", "tensor_add", lo, lo, tmp)
            p.emit("vector", "tensor_sub", tmp, mid, hi)
            p.emit("vector", "tensor_mul", tmp, tmp, ci)
            p.emit("vector", "tensor_add", hi, hi, tmp)
        p.emit("vector", "tensor_single_scalar",
               mrow[0:1, b * 2048:(b + 1) * 2048], rsl, hi, ALU.is_gt)
    c.mm(h0, W1, xn_b, 4096, 1024, 4096, b_src=BF16, act=AF.Gelu,
         bias_col=b1, out_dtype=BF16)
    c.mm(blk, W2, h0, 1024, 4096, 4096, b_src=BF16, bias_col=b2)
    half = c.cconst(0.5, slot=14)
    for n0 in range(0, 4096, 512):
        p.emit("tensor", "matmul", c.pr[:, 0:512], c.onesf[0:1, :],
               mrow[0:1, n0:n0 + 512], start=True, stop=True)
        p.emit("scalar", "copy", c.bc1[:, 0:512], c.pr[:, 0:512])
        p.emit("vector", "tensor_single_scalar", c.mu8[:, 0:512],
               c.bc1[:, 0:512], half, ALU.is_gt)
        for kt in range(8):
            s0 = slice(kt * 128, (kt + 1) * 128)
            p.dma(c.wf[:, 0:512], blk[s0, n0:n0 + 512])
            p.dma(c.wf2[:, 0:512], xn_f[s0, n0:n0 + 512])
            p.emit("vector", "select", c.ev[0][:, 0:512], c.mu8[:, 0:512],
                   c.wf[:, 0:512], c.wf2[:, 0:512])
            p.dma(c.wf[:, 0:512], x[s0, n0:n0 + 512])
            p.dma(c.wf2[:, 0:512], ao[s0, n0:n0 + 512])
            p.emit("vector", "tensor_add", c.ev[0][:, 0:512],
                   c.ev[0][:, 0:512], c.wf[:, 0:512])
            p.emit("vector", "tensor_add", c.ev[0][:, 0:512],
                   c.ev[0][:, 0:512], c.wf2[:, 0:512])
            p.dma(x1[s0, n0:n0 + 512], c.ev[0][:, 0:512])


def emit_l1(p, c, d):
    x = d("x", [1024, 4096], F32, "ExternalInput")
    AqT = d("AqT", [1024, 1024], F32, "ExternalInput")
    AkT = d("AkT", [1024, 1024], F32, "ExternalInput")
    AvT = d("AvT", [1024, 1024], F32, "ExternalInput")
    AoT = d("AoT", [1024, 1024], F32, "ExternalInput")
    bo = d("bo", [1024], F32, "ExternalInput")
    W1 = d("W1", [1024, 4096], F32, "ExternalInput")
    b1 = d("b1", [4096], F32, "ExternalInput")
    W2 = d("W2", [4096, 1024], F32, "ExternalInput")
    b2 = d("b2", [1024], F32, "ExternalInput")
    x1 = d("x1", [1024, 4096], F32, "ExternalOutput")
    xn_f = d("xn_f", [1024, 4096], F32)
    xn_b = d("xn_b", [1024, 4096], BF16)
    qf = d("qf", [1024, 4096], BF16)
    kf = d("kf", [1024, 4096], BF16)
    vt = d("vt", [4096, 1024], BF16)
    yb = d("yb", [1024, 4096], BF16)
    ao = d("ao", [1024, 4096], F32)
    h0 = d("h0", [4096, 4096], BF16)
    f1 = d("f1", [1024, 4096], F32)
    c.ln_ff(x, xn_f, xn_b)
    c.mm(qf, AqT, xn_b, 1024, 1024, 4096, b_src=BF16, act_scale=0.1,
         out_dtype=BF16)
    c.mm(kf, AkT, xn_b, 1024, 1024, 4096, b_src=BF16, act_scale=0.1,
         out_dtype=BF16)
    c.mm(vt, xn_b, AvT, 4096, 1024, 1024, a_src=BF16, out_dtype=BF16)
    for b in range(BPC):
        for kd in range(8):
            for kt in range(16):
                p.dma(c.Bc[:, (kd * 16 + kt) * 128:(kd * 16 + kt + 1) * 128],
                      kf[kd * 128:(kd + 1) * 128,
                         b * 2048 + kt * 128: b * 2048 + (kt + 1) * 128])
        for j in range(4):
            for kd in range(8):
                p.dma(c.Ar[:, kd * 512:(kd + 1) * 512],
                      qf[kd * 128:(kd + 1) * 128,
                         b * 2048 + j * 512: b * 2048 + (j + 1) * 512])
            for kt in range(16):
                pt = c.Bc[:, 16384 + kt * 512: 16384 + (kt + 1) * 512]
                if kt > 4 * j + 3:
                    p.emit("vector", "memset", pt, -1.0)
                    continue
                for kd in range(8):
                    p.emit("tensor", "matmul", c.ps[2][:, 0:512],
                           c.Bc[:, (kd * 16 + kt) * 128:(kd * 16 + kt + 1) * 128],
                           c.Ar[:, kd * 512:(kd + 1) * 512],
                           start=(kd == 0), stop=(kd == 7), wait=(kd == 0))
                if 4 * j <= kt <= 4 * j + 3:
                    p.emit("scalar", "copy", c.evb[0][:, 0:512], c.ps[2][:, 0:512])
                    p.emit("gpsimd", "affine_select", pt, c.evb[0][:, 0:512],
                           [[1, 512]], ALU.is_ge, -1.0,
                           base=j * 512 - kt * 128, channel_multiplier=-1)
                else:
                    p.emit("scalar", "copy", pt, c.ps[2][:, 0:512])
            for kt in range(16):
                pt = c.Bc[:, 16384 + kt * 512: 16384 + (kt + 1) * 512]
                p.emit("tensor", "matmul", c.pr[0:1, 0:512], c.onesb[:, 0:1],
                       pt, start=(kt == 0), stop=(kt == 15), wait=(kt == 0))
            den = c.row(0)[0:1, 0:512]
            p.emit("scalar", "copy", den, c.pr[0:1, 0:512])
            rec = c.row(1)[0:1, 0:512]
            c.recip_row(rec, den, 512, signed=True, add=1e-5)
            c.bcast_row(rec, 512, c.bc1)
            for m in range(8):
                for kt in range(16):
                    p.dma(c.scrb[:, kt * 128:(kt + 1) * 128],
                          vt[b * 2048 + kt * 128: b * 2048 + (kt + 1) * 128,
                             m * 128:(m + 1) * 128])
                for kt in range(16):
                    pt = c.Bc[:, 16384 + kt * 512: 16384 + (kt + 1) * 512]
                    p.emit("tensor", "matmul", c.ps[3][:, 0:512],
                           c.scrb[:, kt * 128:(kt + 1) * 128], pt,
                           start=(kt == 0), stop=(kt == 15), wait=(kt == 0))
                p.emit("vector", "tensor_mul", c.ev[0][:, 0:512],
                       c.ps[3][:, 0:512], c.bc1[:, 0:512])
                p.emit("vector", "tensor_copy", c.evb[1][:, 0:512],
                       c.ev[0][:, 0:512])
                p.dma(yb[m * 128:(m + 1) * 128,
                         b * 2048 + j * 512: b * 2048 + (j + 1) * 512],
                      c.evb[1][:, 0:512])
    c.mm(ao, AoT, yb, 1024, 1024, 4096, b_src=BF16, bias_col=bo)
    c.mm(h0, W1, xn_b, 4096, 1024, 4096, b_src=BF16, act=AF.Gelu,
         bias_col=b1, out_dtype=BF16)
    c.mm(f1, W2, h0, 1024, 4096, 4096, b_src=BF16, bias_col=b2)
    combine3(c, x1, x, ao, f1)


def emit_l2(p, c, d):
    x = d("x", [1024, 4096], F32, "ExternalInput")
    s1 = d("s1", [1024], F32, "ExternalInput")
    s2 = d("s2", [1024], F32, "ExternalInput")
    gam = d("gam", [1024], F32, "ExternalInput")
    qkvT = d("qkvT", [1024, 3072], F32, "ExternalInput")
    projT = d("projT", [1024, 1024], F32, "ExternalInput")
    projb = d("projb", [1024], F32, "ExternalInput")
    c1T = d("c1T", [1024, 2048], F32, "ExternalInput")
    c1b = d("c1b", [2048], F32, "ExternalInput")
    c2T = d("c2T", [2048, 1024], F32, "ExternalInput")
    c2b = d("c2b", [1024], F32, "ExternalInput")
    wgT = d("wgT", [1024, 3072], I32, "ExternalInput")
    wuT = d("wuT", [1024, 3072], I32, "ExternalInput")
    wdT = d("wdT", [3072, 1024], I32, "ExternalInput")
    x1 = d("x1", [1024, 4096], F32, "ExternalOutput")
    xn_f = d("xn_f", [1024, 4096], F32)
    xn_b = d("xn_b", [1024, 4096], BF16)
    xa_b = d("xa_b", [1024, 4096], BF16)
    q2b = d("q2b", [1024, 4096], BF16)
    k2t = d("k2t", [4096, 1024], BF16)
    v2t = d("v2t", [4096, 1024], BF16)
    atb = d("atb", [1024, 4096], BF16)
    x2f = d("x2f", [1024, 4096], F32)
    x2b = d("x2b", [1024, 4096], BF16)
    xr2 = d("xr2", [1024, 4096], BF16)
    h2 = d("h2", [2048, 4096], BF16)
    a2 = d("a2", [1024, 4096], F32)
    gb = d("gb", [3072, 4096], BF16)
    ub = d("ub", [3072, 4096], BF16)
    hb = d("hb", [3072, 4096], BF16)
    f2 = d("f2", [1024, 4096], F32)
    RMS = 1024 ** 0.5 + 1e-6
    c.ln_ff(x, xn_f, xn_b)
    c.ln_ff(xn_f, None, xa_b, rms_mul=RMS, scale1=s1)

    def phi(ps, m, n_abs, nw, out):
        p.emit("scalar", "square", c.ev[0][:, 0:nw], ps)
        p.emit("scalar", "activation", c.evb[0][:, 0:nw], c.ev[0][:, 0:nw],
               AF.Exp, scale=-0.5)
        p.dma(out[m * 128:(m + 1) * 128, n_abs:n_abs + nw], c.evb[0][:, 0:nw])

    c.mm(None, qkvT, xa_b, 1024, 1024, 4096, b_src=BF16,
         custom=lambda ps, m, n, nw: phi(ps, m, n, nw, q2b))
    c.mm(None, xa_b, qkvT, 4096, 1024, 1024, a_src=BF16, b_off=(0, 1024),
         custom=lambda ps, m, n, nw: phi(ps, m, n, nw, k2t))
    c.mm(v2t, xa_b, qkvT, 4096, 1024, 1024, a_src=BF16, b_off=(0, 2048),
         out_dtype=BF16)
    for b in range(BPC):
        for h in range(16):
            r0, r1 = h * 64, (h + 1) * 64
            for kt in range(16):
                p.dma(c.Ar[:, kt * 64:(kt + 1) * 64],
                      k2t[b * 2048 + kt * 128: b * 2048 + (kt + 1) * 128, r0:r1])
                p.dma(c.scrb[:, kt * 64:(kt + 1) * 64],
                      v2t[b * 2048 + kt * 128: b * 2048 + (kt + 1) * 128, r0:r1])
            for kt in range(16):
                p.emit("tensor", "matmul", c.ps[2][0:64, 0:64],
                       c.Ar[:, kt * 64:(kt + 1) * 64],
                       c.scrb[:, kt * 64:(kt + 1) * 64],
                       start=(kt == 0), stop=(kt == 15), wait=(kt == 0))
            for kt in range(16):
                p.emit("tensor", "matmul", c.ps[3][0:64, 0:1],
                       c.Ar[:, kt * 64:(kt + 1) * 64], c.onesb[:, 0:1],
                       start=(kt == 0), stop=(kt == 15), wait=(kt == 0))
            p.emit("scalar", "copy", c.Ar[0:64, 1024:1088], c.ps[2][0:64, 0:64])
            p.emit("scalar", "copy", c.Ar[0:64, 1088:1089], c.ps[3][0:64, 0:1])
            for j in range(4):
                p.dma(c.Ar[0:64, 2048:2560],
                      q2b[r0:r1, b * 2048 + j * 512: b * 2048 + (j + 1) * 512])
                qa = c.Ar[0:64, 2048:2560]
                p.emit("tensor", "matmul", c.pr[0:1, 0:512],
                       c.Ar[0:64, 1088:1089], qa, start=True, stop=True)
                den = c.row(0)[0:1, 0:512]
                p.emit("scalar", "copy", den, c.pr[0:1, 0:512])
                rec = c.row(1)[0:1, 0:512]
                c.recip_row(rec, den, 512, add=1e-6)
                c.bcast_row(rec, 512, c.bc1)
                p.emit("tensor", "matmul", c.ps[2][0:64, 0:512],
                       c.Ar[0:64, 1024:1088], qa, start=True, stop=True)
                p.emit("vector", "tensor_mul", c.ev[0][0:64, 0:512],
                       c.ps[2][0:64, 0:512], c.bc1[0:64, 0:512])
                p.emit("vector", "tensor_copy", c.evb[1][0:64, 0:512],
                       c.ev[0][0:64, 0:512])
                p.dma(atb[r0:r1, b * 2048 + j * 512: b * 2048 + (j + 1) * 512],
                      c.evb[1][0:64, 0:512])
    pj = d("pj", [1024, 4096], F32)
    c.mm(pj, projT, atb, 1024, 1024, 4096, b_src=BF16, bias_col=projb)
    for kt in range(8):
        gcol = c.bias_col_load(gam, kt, slot=3)
        for n0 in range(0, 4096, 512):
            s0 = slice(kt * 128, (kt + 1) * 128)
            p.dma(c.wf[:, 0:512], pj[s0, n0:n0 + 512])
            p.dma(c.wf2[:, 0:512], xn_f[s0, n0:n0 + 512])
            p.emit("vector", "scalar_tensor_tensor", c.ev[0][:, 0:512],
                   c.wf[:, 0:512], gcol, c.wf2[:, 0:512], ALU.mult, ALU.add)
            p.dma(x2f[s0, n0:n0 + 512], c.ev[0][:, 0:512])
            p.emit("vector", "tensor_copy", c.evb[0][:, 0:512], c.ev[0][:, 0:512])
            p.dma(x2b[s0, n0:n0 + 512], c.evb[0][:, 0:512])
    c.ln_ff(x2f, None, xr2, rms_mul=RMS, scale1=s2)
    c.mm(h2, c1T, xr2, 2048, 1024, 4096, b_src=BF16, act=AF.Gelu,
         bias_col=c1b, out_dtype=BF16)
    mlp = d("mlp", [1024, 4096], F32)
    c.mm(mlp, c2T, h2, 1024, 2048, 4096, b_src=BF16, bias_col=c2b)
    for kt in range(8):
        gcol = c.bias_col_load(gam, kt, slot=3)
        for n0 in range(0, 4096, 512):
            s0 = slice(kt * 128, (kt + 1) * 128)
            p.dma(c.wf[:, 0:512], mlp[s0, n0:n0 + 512])
            p.dma(c.wf2[:, 0:512], x2f[s0, n0:n0 + 512])
            p.emit("vector", "scalar_tensor_tensor", c.ev[0][:, 0:512],
                   c.wf[:, 0:512], gcol, c.wf2[:, 0:512], ALU.mult, ALU.add)
            p.dma(a2[s0, n0:n0 + 512], c.ev[0][:, 0:512])
    SC = 1.0 / 127.0
    c.mm(gb, wgT, xn_b, 3072, 1024, 4096, a_src=I32, a_scale=SC, b_src=BF16,
         act=AF.Silu, out_dtype=BF16)
    c.mm(ub, wuT, xn_b, 3072, 1024, 4096, a_src=I32, a_scale=SC, b_src=BF16,
         out_dtype=BF16)
    for kt in range(24):
        for n0 in range(0, 4096, 512):
            s0 = slice(kt * 128, (kt + 1) * 128)
            p.dma(c.evb[0][:, 0:512], gb[s0, n0:n0 + 512])
            p.dma(c.evb[1][:, 0:512], ub[s0, n0:n0 + 512])
            p.emit("vector", "tensor_mul", c.evb[0][:, 0:512],
                   c.evb[0][:, 0:512], c.evb[1][:, 0:512])
            p.dma(hb[s0, n0:n0 + 512], c.evb[0][:, 0:512])
    c.mm(f2, wdT, hb, 1024, 3072, 4096, a_src=I32, a_scale=SC, b_src=BF16)
    combine3(c, x1, x, a2, f2)


def emit_l3a(p, c, d):
    x = d("x", [1024, 4096], F32, "ExternalInput")
    AqT = d("AqT", [1024, 1024], F32, "ExternalInput")
    AkT = d("AkT", [1024, 1024], F32, "ExternalInput")
    AvT = d("AvT", [1024, 1024], F32, "ExternalInput")
    R128 = d("R128", [128, 128], F32, "ExternalInput")
    cosT = d("cosT", [128, 2048], F32, "ExternalInput")
    sinT = d("sinT", [128, 2048], F32, "ExternalInput")
    a3 = d("a3", [1024, 4096], F32, "ExternalOutput")
    xn_f = d("xn_f", [1024, 4096], F32, "ExternalOutput")
    xn_b = d("xn_b", [1024, 4096], BF16, "ExternalOutput")
    qb = d("qb", [1024, 4096], BF16)
    kb = d("kb", [1024, 4096], BF16)
    v3 = d("v3", [4096, 1024], BF16)
    c.ln_ff(x, xn_f, xn_b)
    c.mm(qb, AqT, xn_b, 1024, 1024, 4096, b_src=BF16, out_dtype=BF16)
    c.mm(kb, AkT, xn_b, 1024, 1024, 4096, b_src=BF16, out_dtype=BF16)
    c.mm(v3, xn_b, AvT, 4096, 1024, 1024, a_src=BF16, out_dtype=BF16)
    Rb = c.Bc[:, 30720:30848]
    cosb = c.Bc[:, 24576:26624]
    sinb = c.Bc[:, 26624:28672]
    p.emit("gpsimd", "dma_start", Rb, R128[:, :], dma=True)
    p.emit("gpsimd", "dma_start", cosb, cosT[:, :], dma=True)
    p.emit("gpsimd", "dma_start", sinb, sinT[:, :], dma=True)
    for tgt in (qb, kb):
        for kt in range(8):
            for n0 in range(0, 4096, 512):
                pos0 = n0 % 2048
                s0 = slice(kt * 128, (kt + 1) * 128)
                p.dma(c.evb[0][:, 0:512], tgt[s0, n0:n0 + 512])
                p.emit("tensor", "matmul", c.ps[2][:, 0:512], Rb,
                       c.evb[0][:, 0:512], start=True, stop=True)
                p.emit("vector", "tensor_mul", c.ev[1][:, 0:512],
                       c.evb[0][:, 0:512], cosb[:, pos0:pos0 + 512])
                p.emit("vector", "tensor_mul", c.ev[0][:, 0:512],
                       c.ps[2][:, 0:512], sinb[:, pos0:pos0 + 512])
                p.emit("vector", "tensor_add", c.ev[1][:, 0:512],
                       c.ev[1][:, 0:512], c.ev[0][:, 0:512])
                p.emit("vector", "tensor_copy", c.evb[1][:, 0:512],
                       c.ev[1][:, 0:512])
                p.dma(tgt[s0, n0:n0 + 512], c.evb[1][:, 0:512])
    attn_headloop(c, qb, kb, v3, a3, causal=False, evac="exp",
                  out_dtype=F32, recip_add=None, recip_signed=False,
                  scale=0.125)


def emit_l3f(p, c, d, e0, e1, first, last, fused=False):
    xn_f = d("xn_f", [1024, 4096], F32, "ExternalInput")
    xn_b = d("xn_b", [1024, 4096], BF16, "ExternalInput")
    cenT = d("cenT", [1024, 8], F32, "ExternalInput")
    lng = d("lng", [8, 1024], F32, "ExternalInput")
    lnb = d("lnb", [8, 1024], F32, "ExternalInput")
    lng3 = lng.rearrange("e (p q) -> e p q", q=1)
    lnb3 = lnb.rearrange("e (p q) -> e p q", q=1)
    ws = {}
    for e in range(e0, e1):
        ws[e] = (d(f"w1T_{e}", [1024, 4096], F32, "ExternalInput"),
                 d(f"b1_{e}", [4096], F32, "ExternalInput"),
                 d(f"w2T_{e}", [4096, 1024], F32, "ExternalInput"),
                 d(f"b2_{e}", [1024], F32, "ExternalInput"))
    alpha = d("alpha", [4096, 1], F32, "ExternalInput" if not first else "ExternalOutput")
    mask = d("mask", [8, 4096], F32, "ExternalInput" if not first else "ExternalOutput")
    mask3 = mask.rearrange("e (p q) -> e p q", q=1)
    xh_b = d("xh_b", [1024, 4096], BF16, "ExternalInput" if not first else "ExternalOutput")
    z = d("z", [4096, 1024], F32, "ExternalOutput")
    z_in = None if first else d("z_prev", [4096, 1024], F32, "ExternalInput")
    if first:
        def rout(ps, m, n_abs, nw):
            p.emit("scalar", "copy", c.ev[0][:, 0:8], ps)
            p.emit("vector", "reduce_max", c.colf[:, 3:4], c.ev[0][:, 0:8], AX.X)
            p.emit("scalar", "activation", c.ev[1][:, 0:1], c.colf[:, 3:4],
                   AF.Sigmoid)
            p.dma(alpha[m * 128:(m + 1) * 128, 0:1], c.ev[1][:, 0:1])
            for e in range(8):
                p.emit("vector", "tensor_tensor", c.ev[2][:, e:e + 1],
                       c.ev[0][:, e:e + 1], c.colf[:, 3:4], ALU.is_equal)
                p.dma(mask3[e, m * 128:(m + 1) * 128, :], c.ev[2][:, e:e + 1])
        c.mm(None, xn_b, cenT, 4096, 1024, 8, a_src=BF16, custom=rout)
        c.ln_ff(xn_f, None, xh_b)
    xhe = d("xhe", [1024, 4096], BF16)
    he = d("he", [4096, 4096], BF16)
    for e in range(e0, e1):
        w1T, b1, w2T, b2 = ws[e]
        for kt in range(8):
            gcol = c.colf[:, 4:5]
            bcol = c.colf[:, 5:6]
            p.dma(gcol, lng3[e, kt * 128:(kt + 1) * 128, :])
            p.dma(bcol, lnb3[e, kt * 128:(kt + 1) * 128, :])
            for n0 in range(0, 4096, 512):
                s0 = slice(kt * 128, (kt + 1) * 128)
                p.dma(c.evb[0][:, 0:512], xh_b[s0, n0:n0 + 512])
                p.emit("vector", "tensor_scalar", c.evb[1][:, 0:512],
                       c.evb[0][:, 0:512], gcol, bcol, ALU.mult, ALU.add)
                p.dma(xhe[s0, n0:n0 + 512], c.evb[1][:, 0:512])
        c.mm(he, w1T, xhe, 4096, 1024, 4096, b_src=BF16, act=AF.Relu,
             bias_col=b1, out_dtype=BF16)

        def zupd(ps, m, n_abs, nw, e=e):
            mcol = c.colf[:, 6:7]
            p.dma(mcol, mask3[e, m * 128:(m + 1) * 128, :])
            p.emit("scalar", "copy", c.ev[0][:, 0:nw], ps)
            if e == 0:
                p.emit("vector", "tensor_scalar", c.ev[1][:, 0:nw],
                       c.ev[0][:, 0:nw], mcol, None, ALU.mult)
            else:
                zsrc = z_in if e == e0 else z
                p.dma(c.ev[1][:, 0:nw], zsrc[m * 128:(m + 1) * 128, n_abs:n_abs + nw])
                p.emit("vector", "scalar_tensor_tensor", c.ev[1][:, 0:nw],
                       c.ev[0][:, 0:nw], mcol, c.ev[1][:, 0:nw],
                       ALU.mult, ALU.add)
            p.dma(z[m * 128:(m + 1) * 128, n_abs:n_abs + nw], c.ev[1][:, 0:nw])
        c.mm(None, he, w2T, 4096, 4096, 1024, a_src=BF16, bias_row=b2,
             custom=zupd)
    if last and fused:
        idn = d("idn", [128, 128], F32, "ExternalInput")
        x3f = d("x3f", [1024, 4096], F32, "ExternalInput")
        a3f = d("a3f", [1024, 4096], F32, "ExternalInput")
        zT = d("zT", [1024, 4096], F32)
        x4f = d("x4f", [1024, 4096], F32, "ExternalOutput")
        idns = p.sbuf("idns", [128, 128], F32)
        p.dma(idns[:, :], idn[:, :])
        for jt in range(32):
            for kt in range(8):
                p.dma(c.wf[:, 0:128], z[jt * 128:(jt + 1) * 128,
                                        kt * 128:(kt + 1) * 128])
                p.emit("tensor", "transpose", c.ps[2][:, 0:128],
                       c.wf[:, 0:128], idns[:, :])
                p.emit("scalar", "copy", c.ev[0][:, 0:128], c.ps[2][:, 0:128])
                p.dma(zT[kt * 128:(kt + 1) * 128, jt * 128:(jt + 1) * 128],
                      c.ev[0][:, 0:128])
        arow = c.row(0)
        p.dma(arow, alpha.rearrange("a b -> b a"))
        c.bcast_row(arow, 4096, c.bc2)
        for kt in range(8):
            for n0 in range(0, 4096, 512):
                s0 = slice(kt * 128, (kt + 1) * 128)
                p.dma(c.wf[:, 0:512], zT[s0, n0:n0 + 512])
                p.emit("vector", "tensor_mul", c.ev[0][:, 0:512],
                       c.wf[:, 0:512], c.bc2[:, n0:n0 + 512])
                p.dma(c.wf[:, 0:512], xn_f[s0, n0:n0 + 512])
                p.dma(c.wf2[:, 0:512], x3f[s0, n0:n0 + 512])
                p.dma(c.ev[1][:, 0:512], a3f[s0, n0:n0 + 512])
                p.emit("vector", "tensor_add", c.ev[0][:, 0:512],
                       c.ev[0][:, 0:512], c.wf[:, 0:512])
                p.emit("vector", "tensor_add", c.ev[0][:, 0:512],
                       c.ev[0][:, 0:512], c.wf2[:, 0:512])
                p.emit("vector", "tensor_add", c.ev[0][:, 0:512],
                       c.ev[0][:, 0:512], c.ev[1][:, 0:512])
                p.dma(x4f[s0, n0:n0 + 512], c.ev[0][:, 0:512])
        return
    if last:
        xs_t = d("xs_t", [4096, 1024], F32, "ExternalInput")
        x3_t = d("x3_t", [4096, 1024], F32, "ExternalInput")
        a3_t = d("a3_t", [4096, 1024], F32, "ExternalInput")
        x4 = d("x4", [4096, 1024], F32, "ExternalOutput")
        for m in range(32):
            acol = c.colf[:, 7:8]
            p.dma(acol, alpha[m * 128:(m + 1) * 128, 0:1])
            for n0 in range(0, 1024, 512):
                s0 = slice(m * 128, (m + 1) * 128)
                p.dma(c.wf[:, 0:512], z[s0, n0:n0 + 512])
                p.dma(c.wf2[:, 0:512], xs_t[s0, n0:n0 + 512])
                p.emit("vector", "scalar_tensor_tensor", c.ev[0][:, 0:512],
                       c.wf[:, 0:512], acol, c.wf2[:, 0:512], ALU.mult, ALU.add)
                p.dma(c.wf[:, 0:512], x3_t[s0, n0:n0 + 512])
                p.dma(c.wf2[:, 0:512], a3_t[s0, n0:n0 + 512])
                p.emit("vector", "tensor_add", c.ev[0][:, 0:512],
                       c.ev[0][:, 0:512], c.wf[:, 0:512])
                p.emit("vector", "tensor_add", c.ev[0][:, 0:512],
                       c.ev[0][:, 0:512], c.wf2[:, 0:512])
                p.dma(x4[s0, n0:n0 + 512], c.ev[0][:, 0:512])


def _mk_builder(emitfn, name, **kw):
    def b():
        p = Prog(name)
        d = lambda n, s, t, k="Internal": p.dram(n, s, t, kind=k)
        c = Ctx(p)
        emitfn(p, c, d, **kw)
        return p.finalize()
    return b


build_l0 = _mk_builder(emit_l0, "l0")
build_l1 = _mk_builder(emit_l1, "l1")
build_l2 = _mk_builder(emit_l2, "l2")
build_l3a = _mk_builder(emit_l3a, "l3a")


def build_l3f(e0, e1, first, last):
    return _mk_builder(emit_l3f, f"l3f{e0}", e0=e0, e1=e1, first=first,
                       last=last)()


_FUSED_DEBUG = False


def build_fused():
    p = Prog("fused")
    c = Ctx(p)
    names = {}

    def mk_d(prefix, alias=None, finals=()):
        alias = alias or {}

        def d(n, s, t, k="Internal"):
            nm = alias.get(n, prefix + n)
            if nm in names:
                return names[nm]
            kk = k
            if k == "ExternalOutput" and nm not in finals:
                kk = "Internal"
            ap = p.dram(nm, s, t, kind=kk)
            names[nm] = ap
            return ap
        return d

    finals = ("l3f_x4f",)
    if _FUSED_DEBUG:
        finals = ("l3f_x4f", "l0_x1", "l1_x1", "l2_x1", "l3a_a3", "l3a_xn_f",
                  "l3f_z", "l3f_alpha")
    emit_l0(p, c, mk_d("l0_", finals=finals))
    emit_l1(p, c, mk_d("l1_", alias={"x": "l0_x1"}, finals=finals))
    emit_l2(p, c, mk_d("l2_", alias={"x": "l1_x1"}, finals=finals))
    emit_l3a(p, c, mk_d("l3a_", alias={"x": "l2_x1"}, finals=finals))
    emit_l3f(p, c, mk_d("l3f_", alias={"xn_f": "l3a_xn_f",
                                       "xn_b": "l3a_xn_b",
                                       "x3f": "l2_x1", "a3f": "l3a_a3"},
                        finals=finals),
             0, 8, True, True, fused=True)
    return p.finalize()



_PROG_CACHE = {}
_DBG = {}


def _prog(key, fn):
    if key not in _PROG_CACHE:
        _PROG_CACHE[key] = fn()
    return _PROG_CACHE[key]


def _rope_tables():
    pos = np.arange(2048, dtype=np.float64)
    inv = 10000.0 ** (-2.0 * np.arange(32) / 64.0)
    ang = inv[:, None] * pos[None, :]
    cos64 = np.repeat(np.cos(ang), 2, axis=0).astype(np.float32)
    sin64 = np.repeat(np.sin(ang), 2, axis=0).astype(np.float32)
    return np.tile(cos64, (2, 1)), np.tile(sin64, (2, 1))


def _r128():
    R = np.zeros((128, 128), dtype=np.float32)
    for a in range(0, 128, 2):
        R[a, a + 1] = 1.0
        R[a + 1, a] = -1.0
    return R


def kernel(**inp):
    A = np.ascontiguousarray
    f32 = np.float32
    x = np.asarray(inp["L0_attn_x"], dtype=f32)
    xs = [A(x[2 * c:2 * c + 2].reshape(4096, 1024).T) for c in range(CORES)]
    g = lambda n: np.asarray(inp[n], dtype=f32)
    cosT, sinT = _rope_tables()
    shared = {}
    for k, v in dict(AqT=A(g("L0_attn_Wq").T), AkT=A(g("L0_attn_Wk").T),
                     AvT=A(g("L0_attn_Wv").T), AoT=A(g("L0_attn_Wo").T),
                     bq=g("L0_attn_bq"), bk=g("L0_attn_bk"),
                     bv=g("L0_attn_bv"), bo=g("L0_attn_bo"),
                     wr=g("L0_ffn_w_r"), br=g("L0_ffn_b_r"),
                     W1=g("L0_ffn_W1"), b1=g("L0_ffn_b1"),
                     W2=g("L0_ffn_W2"), b2=g("L0_ffn_b2")).items():
        shared["l0_" + k] = v
    for k, v in dict(AqT=A(g("L1_attn_Wq").T), AkT=A(g("L1_attn_Wk").T),
                     AvT=A(g("L1_attn_Wv").T), AoT=A(g("L1_attn_Wo").T),
                     bo=g("L1_attn_bo"), W1=g("L1_ffn_w1"),
                     b1=g("L1_ffn_b1"), W2=g("L1_ffn_w2"),
                     b2=g("L1_ffn_b2")).items():
        shared["l1_" + k] = v
    for k, v in dict(s1=g("L2_attn_scale1"), s2=g("L2_attn_scale2"),
                     gam=g("L2_attn_gamma"), qkvT=A(g("L2_attn_qkv_w").T),
                     projT=A(g("L2_attn_proj_w").T), projb=g("L2_attn_proj_b"),
                     c1T=A(g("L2_attn_conv1_w").T), c1b=g("L2_attn_conv1_b"),
                     c2T=A(g("L2_attn_conv2_w").T), c2b=g("L2_attn_conv2_b"),
                     wgT=A(np.asarray(inp["L2_ffn_w_gate"], np.int32).T),
                     wuT=A(np.asarray(inp["L2_ffn_w_up"], np.int32).T),
                     wdT=A(np.asarray(inp["L2_ffn_w_down"], np.int32).T)).items():
        shared["l2_" + k] = v
    for k, v in dict(AqT=A(g("L3_attn_Wq").T), AkT=A(g("L3_attn_Wk").T),
                     AvT=A(g("L3_attn_Wv").T), R128=_r128(), cosT=cosT,
                     sinT=sinT).items():
        shared["l3a_" + k] = v
    shared["l3f_cenT"] = A(g("L3_ffn_centroids").T)
    shared["l3f_lng"] = g("L3_ffn_ln_g")
    shared["l3f_lnb"] = g("L3_ffn_ln_b")
    shared["l3f_idn"] = np.eye(128, dtype=f32)
    w1e = np.asarray(inp["L3_ffn_w1"], f32)
    w2e = np.asarray(inp["L3_ffn_w2"], f32)
    b1e = np.asarray(inp["L3_ffn_b1"], f32)
    b2e = np.asarray(inp["L3_ffn_b2"], f32)
    for e in range(8):
        shared[f"l3f_w1T_{e}"] = A(w1e[e].T)
        shared[f"l3f_b1_{e}"] = b1e[e]
        shared[f"l3f_w2T_{e}"] = A(w2e[e].T)
        shared[f"l3f_b2_{e}"] = b2e[e]
    nc = _prog("fused", build_fused)
    maps = [{**shared, "l0_x": xs[c]} for c in range(CORES)]
    res = run_bass_kernel_spmd(nc, maps, core_ids=list(range(CORES)))
    if _FUSED_DEBUG:
        _DBG.update({k: [res.results[c][k] for c in range(CORES)]
                     for k in res.results[0].keys()})
    outs = [np.asarray(res.results[c]["l3f_x4f"], f32).T.reshape(2, 2048, 1024)
            for c in range(CORES)]
    return np.concatenate(outs, axis=0).astype(np.float32)
